# version 09
# baseline (speedup 1.0000x reference)
import numpy as np
import ml_dtypes
from contextlib import ExitStack

import concourse.bass as bass
import concourse.mybir as mybir
from concourse.bass import IndirectOffsetOnAxis
from concourse.bass_utils import run_bass_kernel_spmd

F32 = mybir.dt.float32
BF16 = mybir.dt.bfloat16
I32 = mybir.dt.int32
U32 = mybir.dt.uint32
AF = mybir.ActivationFunctionType
ALU = mybir.AluOpType
AX = mybir.AxisListType

NCORES = 8
L = 4096
T = 1024
TT = T // 128
HT = L // 128
NF = 2049
FP = 2176
FT = FP // 128
IF = 14336
IT = IF // 128
CB = 1024
EPS = 1e-5

AF_XBUF = 0
AF_XNBUF = 4096
AF_WBUF = (8192, 9216, 10240, 11264)
AF_SVF = 12288
AF_SQ = (13312, 13824, 14336, 14848)
AF_ROLL = (0, 4096)
AF_ACC = (8192, 12288)
AF_SA = (0, 1024)
AF_DNB = 0
AF_XCK = (4096, 4608)
AF_ACK = (5120, 5632)
AF_OB = (6144, 6656)

B1_XNT = 0
B1_P = 32768
B1_USL = (32768, 40960)
B1_DB = (49152, 49664)

B2_CS = (0, 8192)
B2_DSL = 0
B2_GSL = (0, 8192)
B2_WD = (0, 512)
B2_DLD = (1024, 2048)



def _build_tables():
    k = np.arange(L)
    cosv = np.cos(2 * np.pi * k / L)
    sinv = np.sin(2 * np.pi * k / L)
    h = np.arange(L)[:, None]
    f = np.arange(NF)[None, :]
    m = (h * f) % L
    ctab = np.zeros((L, 2 * FP), dtype=np.float32)
    ctab[:, :NF] = cosv[m]
    ctab[:, FP:FP + NF] = sinv[m]
    ctab_bf = ctab.astype(ml_dtypes.bfloat16)

    fr = np.arange(NF)[:, None]
    d = np.arange(L)[None, :]
    m2 = (fr * d) % L
    coef = np.full((NF, 1), 2.0 / L)
    coef[0, 0] = 1.0 / L
    coef[NF - 1, 0] = 1.0 / L
    dtab = np.zeros((FP, L), dtype=np.float32)
    dtab[:NF, :] = coef * cosv[m2]
    dtab_bf = dtab.astype(ml_dtypes.bfloat16)

    ident = np.eye(128, dtype=np.float32)
    p = np.arange(128, dtype=np.int64)[:, None]
    j = np.arange(TT, dtype=np.int64)[None, :]
    rb = ((j * 128 + p) * (2 * L)).astype(np.int32)
    rball = np.repeat(rb, 8, axis=1)
    return ctab_bf, dtab_bf, ident, rball


def build_nc():
    nc = bass.Bass("TRN2", target_bir_lowering=False)

    x_in = nc.dram_tensor("x", [T, L], F32, kind="ExternalInput").ap()
    gw = nc.dram_tensor("gw", [IF, L], F32, kind="ExternalInput").ap()
    gsu = nc.dram_tensor("gsu", [IF, 4], F32, kind="ExternalInput").ap()
    gsv = nc.dram_tensor("gsv", [4, L], F32, kind="ExternalInput").ap()
    uw = nc.dram_tensor("uw", [IF, L], F32, kind="ExternalInput").ap()
    usu = nc.dram_tensor("usu", [IF, 4], F32, kind="ExternalInput").ap()
    usv = nc.dram_tensor("usv", [4, L], F32, kind="ExternalInput").ap()
    dw = nc.dram_tensor("dw", [L, IF], F32, kind="ExternalInput").ap()
    dsu = nc.dram_tensor("dsu", [L, 4], F32, kind="ExternalInput").ap()
    dsv = nc.dram_tensor("dsv", [4, IF], F32, kind="ExternalInput").ap()
    ctab = nc.dram_tensor("ctab", [L, 2 * FP], BF16, kind="ExternalInput").ap()
    dtab = nc.dram_tensor("dtab", [FP, L], BF16, kind="ExternalInput").ap()
    identd = nc.dram_tensor("identd", [128, 128], F32, kind="ExternalInput").ap()
    rbd = nc.dram_tensor("rbd", [128, 8 * TT], I32, kind="ExternalInput").ap()
    y_out = nc.dram_tensor("y", [T, L], F32, kind="ExternalOutput").ap()

    xn2 = nc.dram_tensor("xn2", [T, 2 * L], F32, kind="Internal").ap()
    attn = nc.dram_tensor("attn", [T, L], F32, kind="Internal").ap()
    xnT_sp = nc.dram_tensor("xnT_sp", [L, T], BF16, kind="Internal").ap()
    dscr = nc.dram_tensor("dscr", [IF, T], BF16, kind="Internal").ap()
    wgl = nc.dram_tensor("wgl", [IF, L], BF16, kind="Internal")
    wul = nc.dram_tensor("wul", [IF, L], BF16, kind="Internal")
    wdl = nc.dram_tensor("wdl", [L, IF], BF16, kind="Internal")

    ctx = ExitStack()
    with ctx:
        ident_s = ctx.enter_context(nc.sbuf_tensor("ident_s", [128, 128], F32))
        rb_s = ctx.enter_context(nc.sbuf_tensor("rb_s", [128, 64], I32))
        sus = ctx.enter_context(nc.sbuf_tensor("sus", [128, 1024], F32))
        suTj = ctx.enter_context(nc.sbuf_tensor("suTj", [4, 256], BF16))
        svb = ctx.enter_context(nc.sbuf_tensor("svb", [4, 1024], BF16))
        wout = ctx.enter_context(nc.sbuf_tensor("wout", [128, 2048], BF16))
        dly_u = ctx.enter_context(nc.sbuf_tensor("dly_u", [128, 64], U32))
        dly_i = ctx.enter_context(nc.sbuf_tensor("dly_i", [128, 64], I32))
        gidx = ctx.enter_context(nc.sbuf_tensor("gidx", [128, 64], I32))
        w8 = ctx.enter_context(nc.sbuf_tensor("w8", [128, 64], F32))
        stat = ctx.enter_context(nc.sbuf_tensor("stat", [128, 64], F32))
        v8b = ctx.enter_context(nc.sbuf_tensor("v8b", [128, 8], BF16))
        arF = ctx.enter_context(nc.sbuf_tensor("arF", [128, 16384], F32))
        arB1 = ctx.enter_context(nc.sbuf_tensor("arB1", [128, 50176], BF16))
        arB2 = ctx.enter_context(nc.sbuf_tensor("arB2", [128, 16384], BF16))
        pgA = ctx.enter_context(nc.psum_tensor("pgA", [128, 2048], F32))
        pgB = ctx.enter_context(nc.psum_tensor("pgB", [128, 2048], F32))

        qS = [ctx.enter_context(nc.semaphore("qS")), 0]
        qA = [ctx.enter_context(nc.semaphore("qA")), 0]
        qG = [ctx.enter_context(nc.semaphore("qG")), 0]
        qY = [ctx.enter_context(nc.semaphore("qY")), 0]
        sV = [ctx.enter_context(nc.semaphore("sV")), 0]
        sP = [ctx.enter_context(nc.semaphore("sP")), 0]
        sC = [ctx.enter_context(nc.semaphore("sC")), 0]
        sPB = [ctx.enter_context(nc.semaphore("sPB")), 0]
        sVB = [ctx.enter_context(nc.semaphore("sVB")), 0]
        qP = [ctx.enter_context(nc.semaphore("qP")), 0]

        block = ctx.enter_context(nc.Block())

        mats = [
            ("g", gw, gsu, gsv, IT, L, wgl),
            ("u", uw, usu, usv, IT, L, wul),
            ("d", dw, dsu, dsv, HT, IF, wdl),
        ]
        suoff = [0, 4 * IT, 8 * IT]

        bplan = []
        for mi, (mn, w_, su_, sv_, RT, C, loc) in enumerate(mats):
            for cb in range(C // CB):
                for rt in range(RT):
                    bplan.append((mi, cb, rt, rt == 0))
        NB = len(bplan)
        NGU = 2 * (L // CB) * IT

        b_qs = [0] * NB
        b_pe = [0] * NB
        b_vm = [0] * NB
        b_vt = [0] * NB
        b_st = [0] * NB
        b_svb = [0] * NB
        b_svbq = [0] * NB
        def bslots(i):
            return AF_WBUF[i % 4], (i % 2) * 1024, (i % 2) * 1024, (i % 2) * 128

        bst = {"i": 0}

        def build_sync(q, n):
            i0 = bst["i"]
            for i in range(i0, min(i0 + n, NB)):
                mi, cb, rt, newcb = bplan[i]
                mn, w_, su_, sv_, RT, C, loc = mats[mi]
                if i == NGU:
                    q.wait_ge(sV[0], marks["v_acc_end"])
                if i >= 4:
                    q.wait_ge(sVB[0], b_vm[i - 4])
                wl = bslots(i)[0]
                q.dma_start(out=arF[:, wl:wl + CB],
                            in_=w_[rt * 128:(rt + 1) * 128, cb * CB:(cb + 1) * CB]
                            ).then_inc(qS[0], 16)
                qS[1] += 16
                b_qs[i] = qS[1]

        def build_scalar(q, n):
            i0 = bst["i"]
            for i in range(i0, min(i0 + n, NB)):
                mi, cb, rt, newcb = bplan[i]
                mn, w_, su_, sv_, RT, C, loc = mats[mi]
                if newcb:
                    if i > 0:
                        q.wait_ge(sVB[0], b_svb[i - 1])
                    q.dma_start(out=arF[0:4, AF_SVF:AF_SVF + CB],
                                in_=sv_[:, cb * CB:(cb + 1) * CB]).then_inc(qA[0], 16)
                    qA[1] += 16
                    bst["svbq"] = qA[1]
                b_svbq[i] = bst["svbq"]
                q.wait_ge(sVB[0], b_vm[i])
                wo = bslots(i)[1]
                q.dma_start(out=loc.ap()[rt * 128:(rt + 1) * 128, cb * CB:(cb + 1) * CB],
                            in_=wout[:, wo:wo + CB]).then_inc(qA[0], 16)
                qA[1] += 16
                b_st[i] = qA[1]

        def emit_tr(t, j):
            mi, cb, rt, newcb = bplan[j]
            gb, so = (j % 2) * 1024, (j % 2) * 128
            if j >= 2:
                t.wait_ge(sVB[0], b_vm[j - 2])
            t.transpose(out=pgB[0:4, gb:gb + 128],
                        in_=sus[:, suoff[mi] + rt * 4: suoff[mi] + rt * 4 + 4],
                        identity=ident_s[:]).then_inc(sPB[0], 1)
            sPB[1] += 1
            assert sPB[1] == b_trv[j]

        def build_tensor(t, n):
            i0 = bst["i"]
            for i in range(i0, min(i0 + n, NB)):
                mi, cb, rt, newcb = bplan[i]
                gb, so = bslots(i)[2], bslots(i)[3]
                if i + 1 < NB:
                    emit_tr(t, i + 1)
                t.wait_ge(sVB[0], b_vt[i])
                t.wait_ge(sVB[0], b_svb[i])
                t.wait_ge(qA[0], b_svbq[i])
                svslot = b_svslot[i]
                for sc_ in range(2):
                    ins = t.matmul(out=pgB[:, gb + sc_ * 512: gb + (sc_ + 1) * 512],
                                   lhsT=suTj[0:4, so:so + 128],
                                   rhs=svb[0:4, svslot + sc_ * 512: svslot + (sc_ + 1) * 512],
                                   start=True, stop=True)
                    ins.then_inc(sPB[0], 1)
                    sPB[1] += 1
                assert sPB[1] == b_pe[i]

        def build_vector(v, n):
            i0 = bst["i"]
            for i in range(i0, min(i0 + n, NB)):
                mi, cb, rt, newcb = bplan[i]
                wl, wo, gb, so = bslots(i)
                if newcb:
                    v.wait_ge(qA[0], b_svbq[i])
                    if b_prevpass_pe[i] is not None:
                        v.wait_ge(sPB[0], b_prevpass_pe[i])
                    v.tensor_copy(out=svb[0:4, b_svslot[i]:b_svslot[i] + CB],
                                  in_=arF[0:4, AF_SVF:AF_SVF + CB])
                    v.drain()
                    v.sem_inc(sVB[0], 1)
                    sVB[1] += 1
                v.wait_ge(sPB[0], b_trv[i])
                v.tensor_copy(out=suTj[0:4, so:so + 128],
                              in_=pgB[0:4, gb:gb + 128]).then_inc(sVB[0], 1)
                sVB[1] += 1
                v.wait_ge(sPB[0], b_pe[i])
                v.wait_ge(qS[0], b_qs[i])
                if i >= 2:
                    v.wait_ge(qA[0], b_st[i - 2])
                for sc_ in range(2):
                    ins = v.tensor_mul(out=wout[:, wo + sc_ * 512: wo + (sc_ + 1) * 512],
                                       in0=arF[:, wl + sc_ * 512: wl + (sc_ + 1) * 512],
                                       in1=pgB[:, gb + sc_ * 512: gb + (sc_ + 1) * 512])
                ins.then_inc(sVB[0], 1)
                sVB[1] += 1
                assert sVB[1] == b_vm[i]

        b_svslot = [0] * NB
        b_prevpass_pe = [None] * NB
        pass_last_chunk = []
        pass_of = []
        passno = -1
        for i, (mi, cb, rt, newcb) in enumerate(bplan):
            if newcb:
                passno += 1
                pass_last_chunk.append(i - 1)
            pass_of.append(passno)
        pass_last_chunk.append(NB - 1)
        for i, (mi, cb, rt, newcb) in enumerate(bplan):
            if newcb and pass_of[i] >= 1:
                b_prevpass_pe[i] = ("pass_end", pass_of[i] - 1)

        def take_build(n):
            if bst["i"] >= NB:
                return
            n = min(n, NB - bst["i"])

            @block.sync
            def _(q):
                build_sync(q, n)

            @block.scalar
            def _(s):
                build_scalar(s, n)

            @block.tensor
            def _(t):
                build_tensor(t, n)

            @block.vector
            def _(v):
                build_vector(v, n)

            bst["i"] += n

        marks = {}

        @block.sync
        def _(q):
            q.dma_start(out=ident_s[:], in_=identd).then_inc(qP[0], 16)
            q.dma_start(out=rb_s[:], in_=rbd).then_inc(qP[0], 16)
            for mi, (mn, w_, su_, sv_, RT, C, loc) in enumerate(mats):
                q.dma_start(out=sus[:, suoff[mi]:suoff[mi] + 4 * RT]
                            .rearrange("p (r c) -> p r c", c=4),
                            in_=su_.rearrange("(r p) c -> p r c", p=128)
                            ).then_inc(qP[0], 16)
            qP[1] += 80
        qS_prolog = qP[1]

        b_trv = [0] * NB
        b_trv[0] = 1
        for i in range(1, NB):
            b_trv[i] = 3 * i - 1
        for i in range(NB):
            b_pe[i] = 3 * i + 4 if i + 1 < NB else 3 * i + 3

        @block.tensor
        def _(t):
            t.wait_ge(qP[0], qS_prolog)
            emit_tr(t, 0)
        c_ = 0
        for i, (mi, cb, rt, newcb) in enumerate(bplan):
            if newcb:
                c_ += 1
                b_svb[i] = c_
            else:
                b_svb[i] = b_svb[i - 1]
            c_ += 1
            b_vt[i] = c_
            c_ += 1
            b_vm[i] = c_
        for i, (mi, cb, rt, newcb) in enumerate(bplan):
            if isinstance(b_prevpass_pe[i], tuple):
                b_prevpass_pe[i] = b_pe[pass_last_chunk[b_prevpass_pe[i][1] + 1]]


        sV_1 = sV[1]
        sP_1 = sP[1]
        sC_1 = sC[1]
        q_x = [0] * TT
        v_stat = [sV_1 + 34 * j + 1 for j in range(TT)]
        v_apply = [sV_1 + 34 * j + 2 for j in range(TT)]
        v_evac = [[sV_1 + 34 * j + 2 + jh + 1 for jh in range(HT)] for j in range(TT)]
        c_sqrt = [sC_1 + j + 1 for j in range(TT)]
        p_xp = [[sP_1 + HT * j + jh + 1 for jh in range(HT)] for j in range(TT)]
        a_xn2 = [0] * TT

        def ph1_step(j):
            @block.scalar
            def _(s):
                if j >= 1:
                    s.wait_ge(sV[0], v_apply[j - 1])
                s.dma_start(out=arF[:, AF_XBUF:AF_XBUF + L],
                            in_=x_in[j * 128:(j + 1) * 128, :]).then_inc(qA[0], 16)
                qA[1] += 16
                q_x[j] = qA[1]

            @block.vector
            def _(v):
                v.wait_ge(qA[0], q_x[j])
                for c in range(8):
                    v.bn_stats(out=stat[:, 6 * c: 6 * (c + 1)],
                               in_=arF[:, AF_XBUF + 512 * c: AF_XBUF + 512 * (c + 1)])
                v.drain()
                v.bn_aggr(out=stat[:, 48:50], in_=stat[:, 0:48])
                v.drain()
                v.tensor_scalar_add(stat[:, 50:51], stat[:, 49:50], EPS)
                v.drain()
                v.sem_inc(sV[0], 1)
                sV[1] += 1
                assert sV[1] == v_stat[j]

            @block.scalar
            def _(s):
                s.wait_ge(sV[0], v_stat[j])
                s.activation(out=stat[:, 51:52], in_=stat[:, 50:51],
                             func=AF.Sqrt).then_inc(sC[0], 1)
                sC[1] += 1
                assert sC[1] == c_sqrt[j]

            @block.vector
            def _(v):
                v.wait_ge(sC[0], c_sqrt[j])
                v.reciprocal(out=stat[:, 52:53], in_=stat[:, 51:52])
                v.drain()
                if j >= 1:
                    v.wait_ge(sP[0], p_xp[j - 1][-1])
                    v.wait_ge(qA[0], a_xn2[j - 1])
                v.tensor_scalar(out=arF[:, AF_XNBUF:AF_XNBUF + L],
                                in0=arF[:, AF_XBUF:AF_XBUF + L],
                                scalar1=stat[:, 48:49], scalar2=stat[:, 52:53],
                                op0=ALU.subtract, op1=ALU.mult)
                v.drain()
                v.sem_inc(sV[0], 1)
                sV[1] += 1
                assert sV[1] == v_apply[j]

            @block.scalar
            def _(s):
                s.wait_ge(sV[0], v_apply[j])
                s.dma_start(out=xn2[j * 128:(j + 1) * 128, 0:L],
                            in_=arF[:, AF_XNBUF:AF_XNBUF + L]).then_inc(qA[0], 16)
                s.dma_start(out=xn2[j * 128:(j + 1) * 128, L:2 * L],
                            in_=arF[:, AF_XNBUF:AF_XNBUF + L]).then_inc(qA[0], 16)
                qA[1] += 32
                a_xn2[j] = qA[1]

            @block.tensor
            def _(t):
                t.wait_ge(sV[0], v_apply[j])
                for jh in range(HT):
                    if jh >= 2:
                        t.wait_ge(sV[0], v_evac[j][jh - 2])
                    elif j >= 1:
                        t.wait_ge(sV[0], v_evac[j - 1][HT - 2 + jh])
                    t.transpose(out=pgA[:, (jh % 2) * 1024:(jh % 2) * 1024 + 128],
                                in_=arF[:, AF_XNBUF + jh * 128: AF_XNBUF + (jh + 1) * 128],
                                identity=ident_s[:]).then_inc(sP[0], 1)
                    sP[1] += 1
                    assert sP[1] == p_xp[j][jh]

            @block.vector
            def _(v):
                for jh in range(HT):
                    v.wait_ge(sP[0], p_xp[j][jh])
                    dst = B1_XNT + jh * T + j * 128
                    v.tensor_copy(out=arB1[:, dst:dst + 128],
                                  in_=pgA[:, (jh % 2) * 1024:(jh % 2) * 1024 + 128]
                                  ).then_inc(sV[0], 1)
                    sV[1] += 1
                    assert sV[1] == v_evac[j][jh]

        q_cs = [0] * FT
        pe_2a = [0] * (2 * FT)
        c_sq = [0] * (2 * FT)
        v_P = [0] * (2 * FT)

        def ph2a_step(jf, tg):
            s_idx = jf * 2 + tg
            if tg == 0:
                @block.scalar
                def _(s):
                    if jf >= 2:
                        s.wait_ge(sP[0], pe_2a[(jf - 2) * 2 + 1])
                    cso = B2_CS[jf % 2]
                    for jh in range(HT):
                        s.dma_start(out=arB2[:, cso + jh * 128: cso + (jh + 1) * 128],
                                    in_=ctab[jh * 128:(jh + 1) * 128,
                                             jf * 128:(jf + 1) * 128]).then_inc(qA[0], 16)
                        s.dma_start(out=arB2[:, cso + 4096 + jh * 128: cso + 4096 + (jh + 1) * 128],
                                    in_=ctab[jh * 128:(jh + 1) * 128,
                                             FP + jf * 128:FP + (jf + 1) * 128]).then_inc(qA[0], 16)
                    qA[1] += 16 * 64
                    q_cs[jf] = qA[1]

            @block.tensor
            def _(t):
                if s_idx == 0:
                    t.wait_ge(sV[0], v_evac[TT - 1][HT - 1])
                t.wait_ge(qA[0], q_cs[jf])
                if s_idx >= 2:
                    t.wait_ge(sC[0], c_sq[s_idx - 2])
                cso = B2_CS[jf % 2]
                half = (s_idx % 2) * 1024
                for jh in range(HT):
                    st_ = (jh == 0)
                    sp_ = (jh == HT - 1)
                    rhs = arB1[:, B1_XNT + jh * T + tg * 512:
                               B1_XNT + jh * T + (tg + 1) * 512]
                    ins = t.matmul(out=pgA[:, half:half + 512],
                                   lhsT=arB2[:, cso + jh * 128: cso + (jh + 1) * 128],
                                   rhs=rhs, start=st_, stop=sp_)
                    if sp_:
                        ins.then_inc(sP[0], 1)
                        sP[1] += 1
                    ins = t.matmul(out=pgA[:, half + 512:half + 1024],
                                   lhsT=arB2[:, cso + 4096 + jh * 128: cso + 4096 + (jh + 1) * 128],
                                   rhs=rhs, start=st_, stop=sp_)
                    if sp_:
                        ins.then_inc(sP[0], 1)
                        sP[1] += 1
                pe_2a[s_idx] = sP[1]

            @block.scalar
            def _(s):
                s.wait_ge(sP[0], pe_2a[s_idx])
                if s_idx >= 2:
                    s.wait_ge(sV[0], v_P[s_idx - 2])
                half = (s_idx % 2) * 1024
                sq0 = AF_SQ[(s_idx % 2) * 2]
                sq1 = AF_SQ[(s_idx % 2) * 2 + 1]
                s.activation(out=arF[:, sq0:sq0 + 512], in_=pgA[:, half:half + 512],
                             func=AF.Square)
                ins = s.activation(out=arF[:, sq1:sq1 + 512],
                                   in_=pgA[:, half + 512:half + 1024], func=AF.Square)
                ins.then_inc(sC[0], 1)
                sC[1] += 1
                c_sq[s_idx] = sC[1]

            @block.vector
            def _(v):
                v.wait_ge(sC[0], c_sq[s_idx])
                sq0 = AF_SQ[(s_idx % 2) * 2]
                sq1 = AF_SQ[(s_idx % 2) * 2 + 1]
                po = B1_P + jf * T + tg * 512
                ins = v.tensor_add(out=arB1[:, po:po + 512],
                                   in0=arF[:, sq0:sq0 + 512],
                                   in1=arF[:, sq1:sq1 + 512])
                ins.then_inc(sV[0], 1)
                sV[1] += 1
                v_P[s_idx] = sV[1]

        wsteps = [("ph1", j) for j in range(TT)] + \
                 [("ph2a", jf, tg) for jf in range(FT) for tg in range(2)]
        per = -(-NGU // len(wsteps))
        for st_i, stp in enumerate(wsteps):
            take_build(min(per, max(0, NGU - bst["i"])))
            if stp[0] == "ph1":
                ph1_step(stp[1])
            else:
                ph2a_step(stp[1], stp[2])
        take_build(max(0, NGU - bst["i"]))
        assert bst["i"] == NGU

        @block.scalar
        def _(s):
            s.wait_ge(sP[0], pe_2a[-1])
            s.dma_start(out=xnT_sp.rearrange("(jh p) t -> p jh t", p=128),
                        in_=arB1[:, B1_XNT:B1_XNT + HT * T]
                        .rearrange("p (jh t) -> p jh t", t=T)).then_inc(qA[0], 16)
            qA[1] += 16
        a_spill = qA[1]
        ph2a_pe_end = sP[1]

        sP_2b = sP[1]
        sV_2b = sV[1]
        qA_2b = qA[1]
        pe_corr = [[sP_2b + 8 * dc + jt + 1 for jt in range(TT)] for dc in range(8)]
        v_corr = [[sV_2b + 8 * dc + jt + 1 for jt in range(TT)] for dc in range(8)]
        q_D = [qA_2b + 16 * FT * (dc + 1) for dc in range(8)]

        @block.scalar
        def _(s):
            s.wait_ge(sP[0], ph2a_pe_end)
            for dc in range(8):
                if dc >= 1:
                    s.wait_ge(sP[0], pe_corr[dc - 1][-1])
                for jfr in range(FT):
                    s.dma_start(out=arB2[:, B2_DSL + jfr * 512: B2_DSL + (jfr + 1) * 512],
                                in_=dtab[jfr * 128:(jfr + 1) * 128,
                                         dc * 512:(dc + 1) * 512]).then_inc(qA[0], 16)
                qA[1] += 16 * FT
            assert qA[1] == q_D[-1]

        @block.tensor
        def _(t):
            t.wait_ge(sC[0], c_sq[-1])
            for dc in range(8):
                t.wait_ge(qA[0], q_D[dc])
                for jt in range(TT):
                    idx = 8 * dc + jt
                    if idx >= 2:
                        t.wait_ge(sV[0], v_corr[(idx - 2) // 8][(idx - 2) % 8])
                    half = (jt % 2) * 1024
                    for jfr in range(FT):
                        ins = t.matmul(out=pgA[:, half:half + 512],
                                       lhsT=arB1[:, B1_P + jfr * T + jt * 128:
                                                B1_P + jfr * T + (jt + 1) * 128],
                                       rhs=arB2[:, B2_DSL + jfr * 512: B2_DSL + (jfr + 1) * 512],
                                       start=(jfr == 0), stop=(jfr == FT - 1))
                    ins.then_inc(sP[0], 1)
                    sP[1] += 1
            assert sP[1] == pe_corr[-1][-1]

        @block.vector
        def _(v):
            v.wait_ge(qA[0], a_spill)
            for dc in range(8):
                for jt in range(TT):
                    v.wait_ge(sP[0], pe_corr[dc][jt])
                    dst = B1_XNT + jt * L + dc * 512
                    v.tensor_copy(out=arB1[:, dst:dst + 512],
                                  in_=pgA[:, (jt % 2) * 1024:(jt % 2) * 1024 + 512]
                                  ).then_inc(sV[0], 1)
                    sV[1] += 1
            assert sV[1] == v_corr[-1][-1]

        ph2b_pe_end = pe_corr[-1][-1]

        sV_2c = sV[1]
        v_top = [sV_2c + 2 * (jt + 1) for jt in range(TT)]
        sC_base_2c = sC[1]

        @block.vector
        def _(v):
            for jt in range(TT):
                cr = arB1[:, B1_XNT + jt * L: B1_XNT + (jt + 1) * L]
                v.drain()
                v.max(out=v8b[:], in_=cr)
                v.drain()
                v.max_index(out=dly_u[:, jt * 8:(jt + 1) * 8], in_max=v8b[:], in_values=cr)
                v.tensor_copy(out=stat[:, 0:8], in_=v8b[:])
                v.drain()
                v.tensor_scalar(out=stat[:, 8:16], in0=stat[:, 0:8],
                                scalar1=stat[:, 0:1], scalar2=None, op0=ALU.subtract)
                v.drain()
                v.sem_inc(sV[0], 1)
                sV[1] += 1
                v.wait_ge(sC[0], sC_base_2c + jt + 1)
                v.reduce_sum(out=stat[:, 24:25], in_=stat[:, 16:24], axis=AX.X)
                v.drain()
                v.reciprocal(out=stat[:, 25:26], in_=stat[:, 24:25])
                v.drain()
                v.tensor_scalar(out=w8[:, jt * 8:(jt + 1) * 8], in0=stat[:, 16:24],
                                scalar1=stat[:, 25:26], scalar2=None, op0=ALU.mult)
                v.tensor_copy(out=dly_i[:, jt * 8:(jt + 1) * 8],
                              in_=dly_u[:, jt * 8:(jt + 1) * 8])
                v.drain()
                v.tensor_tensor(out=gidx[:, jt * 8:(jt + 1) * 8],
                                in0=dly_i[:, jt * 8:(jt + 1) * 8],
                                in1=rb_s[:, jt * 8:(jt + 1) * 8],
                                op=ALU.add)
                v.drain()
                v.sem_inc(sV[0], 1)
                sV[1] += 1
                assert sV[1] == v_top[jt]

        @block.scalar
        def _(s):
            for jt in range(TT):
                s.wait_ge(sV[0], v_top[jt] - 1)
                s.activation(out=stat[:, 16:24], in_=stat[:, 8:16],
                             func=AF.Exp).then_inc(sC[0], 1)
                sC[1] += 1

        qG_3 = qG[1]
        sV_3 = sV[1]
        qA_3 = qA[1]
        g_roll = [qG_3 + 16 * (i + 1) for i in range(TT * 8)]
        v_acc = [sV_3 + (i + 1) for i in range(TT * 8)]
        a_attn = [qA_3 + 16 * (jt + 1) for jt in range(TT)]

        @block.gpsimd
        def _(g):
            g.wait_ge(qA[0], a_xn2[-1])
            g.wait_ge(sP[0], p_xp[-1][-1])
            breg = g.to_reg(T * 2 * L - L)
            for jt in range(TT):
                g.wait_ge(sV[0], v_top[jt])
                for k in range(8):
                    i = jt * 8 + k
                    if i >= 2:
                        g.wait_ge(sV[0], v_acc[i - 2])
                    ro = AF_ROLL[i % 2]
                    g.indirect_dma_start(
                        out=arF[:, ro:ro + L],
                        out_offset=None,
                        in_=xn2,
                        in_offset=IndirectOffsetOnAxis(gidx[:, i:i + 1], axis=1),
                        bounds_check=breg,
                        oob_is_err=False,
                    ).then_inc(qG[0], 16)
                    qG[1] += 16
            assert qG[1] == g_roll[-1]

        @block.vector
        def _(v):
            for jt in range(TT):
                for k in range(8):
                    i = jt * 8 + k
                    v.wait_ge(qG[0], g_roll[i])
                    ro = AF_ROLL[i % 2]
                    if k == 0:
                        v.tensor_scalar(out=arF[:, AF_ACC[0]:AF_ACC[0] + L],
                                        in0=arF[:, ro:ro + L],
                                        scalar1=w8[:, i:i + 1], scalar2=None,
                                        op0=ALU.mult)
                    else:
                        if k == 1 and jt >= 1:
                            v.wait_ge(qA[0], a_attn[jt - 1])
                        dsto = AF_ACC[k % 2]
                        srco = AF_ACC[(k + 1) % 2]
                        v.scalar_tensor_tensor(out=arF[:, dsto:dsto + L],
                                               in0=arF[:, ro:ro + L],
                                               scalar=w8[:, i:i + 1],
                                               in1=arF[:, srco:srco + L],
                                               op0=ALU.mult, op1=ALU.add)
                    v.drain()
                    v.sem_inc(sV[0], 1)
                    sV[1] += 1
            assert sV[1] == v_acc[-1]

        marks["v_acc_end"] = v_acc[-1]

        @block.scalar
        def _(s):
            for jt in range(TT):
                s.wait_ge(sV[0], v_acc[jt * 8 + 7])
                s.dma_start(out=attn[jt * 128:(jt + 1) * 128, :],
                            in_=arF[:, AF_ACC[1]:AF_ACC[1] + L]).then_inc(qA[0], 16)
                qA[1] += 16
            assert qA[1] == a_attn[-1]

        @block.scalar
        def _(s):
            s.wait_ge(sV[0], v_top[-1])
            s.dma_start(out=arB1[:, B1_XNT:B1_XNT + HT * T]
                        .rearrange("p (jh t) -> p jh t", t=T),
                        in_=xnT_sp.rearrange("(jh p) t -> p jh t", p=128)
                        ).then_inc(qA[0], 16)
            qA[1] += 16
        a_xnT2 = qA[1]

        NOI = IT
        q_w4 = [0] * (NOI // 2)
        pe_gu = [0] * (2 * NOI)
        c_silu = [0] * (2 * NOI)
        v_d = [0] * (2 * NOI)
        g_d = [0] * (2 * NOI)

        def gate_chunk_store(rt):
            return b_st[3 * IT + rt]

        def up_chunk_store(rt):
            return b_st[4 * IT + 3 * IT + rt]

        def ph4_step(oi, tg):
            s_idx = oi * 2 + tg
            og = oi // 2
            if tg == 0 and oi % 2 == 0:
                @block.sync
                def _(q):
                    q.wait_ge(qA[0], gate_chunk_store(2 * og + 1))
                    q.wait_ge(qA[0], up_chunk_store(2 * og + 1))
                    if og >= 2:
                        q.wait_ge(sP[0], pe_gu[(2 * og - 3) * 2 + 1])
                    if og == 0:
                        q.wait_ge(sP[0], ph2b_pe_end)
                        q.wait_ge(qG[0], g_roll[-1])
                    for mat, full in ((0, wgl), (1, wul)):
                        slo = B2_GSL[og % 2] if mat == 0 else B1_USL[og % 2]
                        dst = arB2 if mat == 0 else arB1
                        for jh in range(HT):
                            q.dma_start(out=dst[:, slo + jh * 256: slo + (jh + 1) * 256],
                                        in_=full.ap()[og * 256:(og + 1) * 256,
                                                      jh * 128:(jh + 1) * 128],
                                        transpose=True).then_inc(qS[0], 16)
                    qS[1] += 16 * 64
                    q_w4[og] = qS[1]

            @block.tensor
            def _(t):
                if s_idx == 0:
                    t.wait_ge(qA[0], a_xnT2)
                t.wait_ge(qS[0], q_w4[og])
                if s_idx >= 2:
                    t.wait_ge(sC[0], c_silu[s_idx - 2])
                    t.wait_ge(sV[0], v_d[s_idx - 2])
                half = (s_idx % 2) * 1024
                for mat in range(2):
                    slo = B2_GSL[og % 2] if mat == 0 else B1_USL[og % 2]
                    src = arB2 if mat == 0 else arB1
                    base = half if mat == 0 else half + 512
                    for jh in range(HT):
                        lh = src[:, slo + jh * 256 + (oi % 2) * 128:
                                 slo + jh * 256 + (oi % 2) * 128 + 128]
                        ins = t.matmul(out=pgA[:, base:base + 512],
                                       lhsT=lh,
                                       rhs=arB1[:, B1_XNT + jh * T + tg * 512:
                                                B1_XNT + jh * T + (tg + 1) * 512],
                                       start=(jh == 0), stop=(jh == HT - 1))
                        if jh == HT - 1:
                            ins.then_inc(sP[0], 1)
                            sP[1] += 1
                pe_gu[s_idx] = sP[1]

            @block.scalar
            def _(s):
                s.wait_ge(sP[0], pe_gu[s_idx])
                if s_idx >= 2:
                    s.wait_ge(sV[0], v_d[s_idx - 2])
                half = (s_idx % 2) * 1024
                sao = AF_SA[s_idx % 2]
                ins = s.activation(out=arF[:, sao:sao + 512],
                                   in_=pgA[:, half:half + 512], func=AF.Silu)
                ins.then_inc(sC[0], 1)
                sC[1] += 1
                c_silu[s_idx] = sC[1]

            @block.vector
            def _(v):
                v.wait_ge(sC[0], c_silu[s_idx])
                if s_idx >= 2:
                    v.wait_ge(qG[0], g_d[s_idx - 2])
                half = (s_idx % 2) * 1024
                sao = AF_SA[s_idx % 2]
                dbo = B1_DB[s_idx % 2]
                ins = v.tensor_mul(out=arB1[:, dbo:dbo + 512],
                                   in0=arF[:, sao:sao + 512],
                                   in1=pgA[:, half + 512:half + 1024])
                ins.then_inc(sV[0], 1)
                sV[1] += 1
                v_d[s_idx] = sV[1]

            @block.gpsimd
            def _(g):
                g.wait_ge(sV[0], v_d[s_idx])
                dbo = B1_DB[s_idx % 2]
                g.dma_start(out=dscr[oi * 128:(oi + 1) * 128, tg * 512:(tg + 1) * 512],
                            in_=arB1[:, dbo:dbo + 512]).then_inc(qG[0], 16)
                qG[1] += 16
                g_d[s_idx] = qG[1]

        for oi in range(NOI):
            for tg in range(2):
                take_build(2)
                ph4_step(oi, tg)
        take_build(NB - bst["i"])
        assert bst["i"] == NB

        qS_5 = qS[1]
        qA_5 = qA[1]
        sP_5 = sP[1]
        sV_5 = sV[1]
        qY_5 = qY[1]
        q_wd = [[qS_5 + 16 * (120 * hc + i + 1) for i in range(IT)] for hc in range(8)]
        q_xc = [[qS_5 + 16 * (120 * hc + IT + jt + 1) for jt in range(TT)] for hc in range(8)]
        a_dl = [[qA_5 + 16 * (120 * hc + i + 1) for i in range(IT)] for hc in range(8)]
        a_ac = [[qA_5 + 16 * (120 * hc + IT + jt + 1) for jt in range(TT)] for hc in range(8)]
        pe_dn = [[sP_5 + IT * hc + i + 1 for i in range(IT)] for hc in range(8)]
        v_ev = [[sV_5 + 16 * hc + jt + 1 for jt in range(TT)] for hc in range(8)]
        v_fin = [[sV_5 + 16 * hc + 8 + jt + 1 for jt in range(TT)] for hc in range(8)]
        y_st = [[qY_5 + 16 * (8 * hc + jt + 1) for jt in range(TT)] for hc in range(8)]
        pg8 = [(pgA, 0), (pgA, 512), (pgA, 1024), (pgA, 1536),
               (pgB, 0), (pgB, 512), (pgB, 1024), (pgB, 1536)]

        @block.sync
        def _(q):
            q.wait_ge(qA[0], b_st[NB - 1])
            q.wait_ge(sVB[0], b_vm[NB - 1])
            q.wait_ge(sP[0], pe_gu[-1])
            for hc in range(8):
                for i in range(IT):
                    gi = IT * hc + i
                    if gi >= 2:
                        q.wait_ge(sP[0], pe_dn[(gi - 2) // IT][(gi - 2) % IT])
                    wo = B2_WD[i % 2]
                    q.dma_start(out=arB2[:, wo:wo + 512],
                                in_=wdl.ap()[hc * 512:(hc + 1) * 512,
                                             i * 128:(i + 1) * 128],
                                transpose=True).then_inc(qS[0], 16)
                    qS[1] += 16
                for jt in range(TT):
                    gi = 8 * hc + jt
                    if gi >= 2:
                        q.wait_ge(sV[0], v_fin[(gi - 2) // 8][(gi - 2) % 8])
                    elif hc == 0:
                        q.wait_ge(sV[0], marks["v_acc_end"])
                    xo = AF_XCK[jt % 2]
                    q.dma_start(out=arF[:, xo:xo + 512],
                                in_=x_in[jt * 128:(jt + 1) * 128, hc * 512:(hc + 1) * 512]
                                ).then_inc(qS[0], 16)
                    qS[1] += 16
            assert qS[1] == q_xc[-1][-1]

        @block.scalar
        def _(s):
            s.wait_ge(qG[0], g_d[-1])
            for hc in range(8):
                for i in range(IT):
                    gi = IT * hc + i
                    if gi >= 2:
                        s.wait_ge(sP[0], pe_dn[(gi - 2) // IT][(gi - 2) % IT])
                    do = B2_DLD[i % 2]
                    s.dma_start(out=arB2[:, do:do + T],
                                in_=dscr[i * 128:(i + 1) * 128, :]).then_inc(qA[0], 16)
                    qA[1] += 16
                for jt in range(TT):
                    gi = 8 * hc + jt
                    if gi >= 2:
                        s.wait_ge(sV[0], v_fin[(gi - 2) // 8][(gi - 2) % 8])
                    ao = AF_ACK[jt % 2]
                    s.dma_start(out=arF[:, ao:ao + 512],
                                in_=attn[jt * 128:(jt + 1) * 128, hc * 512:(hc + 1) * 512]
                                ).then_inc(qA[0], 16)
                    qA[1] += 16
            assert qA[1] == a_ac[-1][-1]

        @block.tensor
        def _(t):
            for hc in range(8):
                for i in range(IT):
                    t.wait_ge(qS[0], q_wd[hc][i])
                    t.wait_ge(qA[0], a_dl[hc][i])
                    if hc >= 1 and i == 0:
                        t.wait_ge(sV[0], v_ev[hc - 1][-1])
                    wo = B2_WD[i % 2]
                    do = B2_DLD[i % 2]
                    for jt in range(TT):
                        ps, off = pg8[jt]
                        ins = t.matmul(out=ps[:, off:off + 512],
                                       lhsT=arB2[:, do + jt * 128: do + (jt + 1) * 128],
                                       rhs=arB2[:, wo:wo + 512],
                                       start=(i == 0), stop=(i == IT - 1))
                    ins.then_inc(sP[0], 1)
                    sP[1] += 1
            assert sP[1] == pe_dn[-1][-1]

        @block.vector
        def _(v):
            for hc in range(8):
                v.wait_ge(sP[0], pe_dn[hc][-1])
                for jt in range(TT):
                    if hc >= 1:
                        v.wait_ge(qY[0], y_st[hc - 1][jt])
                    ps, off = pg8[jt]
                    v.tensor_copy(out=arF[:, AF_DNB + jt * 512: AF_DNB + (jt + 1) * 512],
                                  in_=ps[:, off:off + 512]).then_inc(sV[0], 1)
                    sV[1] += 1
                    assert sV[1] == v_ev[hc][jt]
                for jt in range(TT):
                    v.wait_ge(qS[0], q_xc[hc][jt])
                    v.wait_ge(qA[0], a_ac[hc][jt])
                    xo = AF_XCK[jt % 2]
                    ao = AF_ACK[jt % 2]
                    ob = AF_OB[jt % 2]
                    v.drain()
                    v.tensor_add(out=arF[:, ob:ob + 512],
                                 in0=arF[:, AF_DNB + jt * 512: AF_DNB + (jt + 1) * 512],
                                 in1=arF[:, xo:xo + 512])
                    v.drain()
                    v.tensor_add(out=arF[:, AF_DNB + jt * 512: AF_DNB + (jt + 1) * 512],
                                 in0=arF[:, ob:ob + 512],
                                 in1=arF[:, ao:ao + 512])
                    v.drain()
                    v.sem_inc(sV[0], 1)
                    sV[1] += 1
                    assert sV[1] == v_fin[hc][jt]

        @block.gpsimd
        def _(g):
            for hc in range(8):
                for jt in range(TT):
                    g.wait_ge(sV[0], v_fin[hc][jt])
                    g.dma_start(out=y_out[jt * 128:(jt + 1) * 128, hc * 512:(hc + 1) * 512],
                                in_=arF[:, AF_DNB + jt * 512: AF_DNB + (jt + 1) * 512]
                                ).then_inc(qY[0], 16)
                    qY[1] += 16
            assert qY[1] == y_st[-1][-1]

        @block.sync
        def _(q):
            q.wait_ge(qY[0], y_st[-1][-1])

    return nc


def kernel(L0_attn_x, L0_ffn_gate_w, L0_ffn_gate_su, L0_ffn_gate_sv,
           L0_ffn_up_w, L0_ffn_up_su, L0_ffn_up_sv,
           L0_ffn_down_w, L0_ffn_down_su, L0_ffn_down_sv):
    x = np.ascontiguousarray(np.asarray(L0_attn_x, dtype=np.float32).reshape(NCORES * T, L))
    gwv = np.ascontiguousarray(np.asarray(L0_ffn_gate_w, dtype=np.float32))
    gsuv = np.ascontiguousarray(np.asarray(L0_ffn_gate_su, dtype=np.float32))
    gsvv = np.ascontiguousarray(np.asarray(L0_ffn_gate_sv, dtype=np.float32))
    uwv = np.ascontiguousarray(np.asarray(L0_ffn_up_w, dtype=np.float32))
    usuv = np.ascontiguousarray(np.asarray(L0_ffn_up_su, dtype=np.float32))
    usvv = np.ascontiguousarray(np.asarray(L0_ffn_up_sv, dtype=np.float32))
    dwv = np.ascontiguousarray(np.asarray(L0_ffn_down_w, dtype=np.float32))
    dsuv = np.ascontiguousarray(np.asarray(L0_ffn_down_su, dtype=np.float32))
    dsvv = np.ascontiguousarray(np.asarray(L0_ffn_down_sv, dtype=np.float32))

    ctab, dtab, ident, rball = _build_tables()

    nc = build_nc()
    in_maps = []
    for c in range(NCORES):
        in_maps.append({
            "x": x[c * T:(c + 1) * T],
            "gw": gwv, "gsu": gsuv, "gsv": gsvv,
            "uw": uwv, "usu": usuv, "usv": usvv,
            "dw": dwv, "dsu": dsuv, "dsv": dsvv,
            "ctab": ctab, "dtab": dtab, "identd": ident, "rbd": rball,
        })
    res = run_bass_kernel_spmd(nc, in_maps, core_ids=list(range(NCORES)))
    y = np.concatenate([res.results[c]["y"] for c in range(NCORES)], axis=0)
    return y.reshape(1, NCORES * T, L)
